# version 01 (final)
# baseline (speedup 1.0000x reference)
import os
import contextlib
import numpy as np
import ml_dtypes
import concourse.bass as bass
import concourse.mybir as mybir
from concourse.bass_utils import run_bass_kernel_spmd

F32 = mybir.dt.float32
BF16 = mybir.dt.bfloat16
I32 = mybir.dt.int32
AF = mybir.ActivationFunctionType
ALU = mybir.AluOpType
AX = mybir.AxisListType

B, S, C = 4, 4096, 1024
TL = 2048
NT = 16
FC = 8
HD = 64
NBT = 32
NE = 31
INTER = 1024
FSPAN = 1024
BSPAN = 256
NBSP = TL // BSPAN
CHK = 128
NH, HK = 16, 64
EPS_LN = 1e-5
EPS_GN = 1e-5 * 64
NSUB = int(os.environ.get("KDEV_NSUB", "4"))
SIMSAFE = bool(int(os.environ.get("KDEV_SIMSAFE", "0")))
BSPANS = int(os.environ.get("KDEV_BSPANS", "99"))
BPASS1 = bool(int(os.environ.get("KDEV_BPASS1", "1")))
BCUT = int(os.environ.get("KDEV_BCUT", "99"))
BHCUT = int(os.environ.get("KDEV_BHCUT", "99"))
BPCUT = int(os.environ.get("KDEV_BPCUT", "99"))

_CACHE = {}


class Arena:
    def __init__(self, t, n_bf16):
        self.t = t
        self.n = n_bf16
        self.top = 0

    def reset(self, top=0):
        self.top = top

    def bf16(self, cols, parts=128):
        cols = int(cols)
        off = self.top
        assert off + cols <= self.n, f"arena overflow {off}+{cols}>{self.n}"
        self.top += cols + (cols & 1)
        return self.t[0:parts, off:off + cols]

    def f32(self, cols, parts=128):
        cols = int(cols)
        if self.top & 1:
            self.top += 1
        off = self.top
        assert off + 2 * cols <= self.n, "arena overflow f32"
        self.top += 2 * cols
        return self.t[0:parts, off:off + 2 * cols].bitcast(F32)


def _build():
    nc = bass.Bass("TRN2", target_bir_lowering=False)
    dt = {}

    def din(name, shape, dtype=F32):
        dt[name] = nc.dram_tensor(name, list(shape), dtype, kind="ExternalInput")
        return dt[name]

    din("xb", (S, C))
    din("prm", (1, 8), I32)
    din("umcol", (128, 1), I32)
    din("wqT", (C, HD)); din("wkT", (C, HD)); din("wvT", (C, HD))
    din("ws1", (C, INTER)); din("ws2", (INTER, C))
    din("we1", (NE, C, INTER)); din("we2", (NE, INTER, C))
    din("bs1c", (128, FC)); din("be1c", (128, NE * FC))
    din("bs2r", (1, C)); din("be2", (NE, C))
    din("wrf", (C, NE)); din("brr", (1, NE))
    din("wrT", (C, C)); din("wkbT", (C, C)); din("wvbT", (C, C))
    din("wgT", (C, C)); din("woT", (C, C))
    din("maa_w1", (128, FC * 160))
    din("maa_w2", (32, 5 * C))
    din("td_w1", (128, FC * 64))
    din("td_w2", (64, C))
    for nm in ("tmx", "tmw", "tmk", "tmv", "tmr", "tmg", "tdc"):
        din(nm, (128, FC))
    din("ucol", (128, 8))
    din("lngr", (1, C)); din("lnbr", (1, C))

    y_out = nc.dram_tensor("y", [TL, C], F32, kind="ExternalOutput")
    mail_x = nc.dram_tensor("mail_x", [2, C], F32, kind="Internal",
                            addr_space="Shared")
    mail_s = nc.dram_tensor("mail_s", [2, 128, 8 * HK], F32, kind="Internal",
                            addr_space="Shared")
    mail_f = nc.dram_tensor("mail_f", [2, 2, 16], I32, kind="Internal",
                            addr_space="Shared")

    c_idb = nc.inline_tensor(
        np.eye(128, dtype=np.float32).astype(ml_dtypes.bfloat16), name="c_idb").ap()
    mskut_np = (np.arange(128)[:, None] < np.arange(128)[None, :]).astype(np.float32)
    c_msk = nc.inline_tensor(mskut_np.astype(ml_dtypes.bfloat16), name="c_msk").ap()
    c_onesb = nc.inline_tensor(
        np.ones((1, 128), np.float32).astype(ml_dtypes.bfloat16), name="c_onesb").ap()
    c_onesf = nc.inline_tensor(np.ones((1, 128), np.float32), name="c_onesf").ap()
    c_onescol = nc.inline_tensor(
        np.ones((128, 1), np.float32).astype(ml_dtypes.bfloat16),
        name="c_onescol").ap()

    ctx = contextlib.ExitStack()

    def sbuf(name, shape, dtype=F32):
        return ctx.enter_context(nc.sbuf_tensor(name, list(shape), dtype))

    def psum(name, shape, dtype=F32):
        return ctx.enter_context(nc.psum_tensor(name, list(shape), dtype))

    stream = sbuf("stream", (128, NT * 1024))
    xT = sbuf("xT", (128, FC * (FSPAN + 1)), BF16)
    wA = sbuf("wA", (128, FC * 1024), BF16)
    wB = sbuf("wB", (128, FC * 1024), BF16)
    AR = Arena(sbuf("arena", (128, 38912), BF16), 38912)
    s_idb = sbuf("s_idb", (128, 128), BF16)
    s_msk = sbuf("s_msk", (128, 128), BF16)
    s_ones = sbuf("s_ones", (1, 128), BF16)
    s_onesf = sbuf("s_onesf", (1, 128))
    s_onescol = sbuf("s_onescol", (128, 1), BF16)
    s_wqT = sbuf("s_wqT", (128, FC * HD), BF16)
    s_wkT = sbuf("s_wkT", (128, FC * HD), BF16)
    s_wvT = sbuf("s_wvT", (128, FC * HD), BF16)
    s_bs1 = sbuf("s_bs1", (128, FC))
    s_be1 = sbuf("s_be1", (128, NE * FC))
    s_wrf = sbuf("s_wrf", (128, FC * NE), BF16)
    s_brr = sbuf("s_brr", (1, NE), BF16)
    s_bs2 = sbuf("s_bs2", (1, C), BF16)
    s_be2 = sbuf("s_be2", (NE, C), BF16)
    s_tm = {nm: sbuf("s_" + nm, (128, FC)) for nm in
            ("tmx", "tmw", "tmk", "tmv", "tmr", "tmg", "tdc")}
    s_ucol = sbuf("s_ucol", (128, 8))
    s_prm = sbuf("s_prm", (1, 8), I32)
    s_um = sbuf("s_um", (128, 1), I32)
    s_flag = sbuf("s_flag", (1, 16), I32)
    s_lastcol = sbuf("s_lastcol", (128, FC), BF16)
    s_S = sbuf("s_S", (128, 8 * HK))
    s_srx = sbuf("s_srx", (128, 8 * HK))
    st_sum = sbuf("st_sum", (128, 16)); st_sq = sbuf("st_sq", (128, 16))
    st_mean = sbuf("st_mean", (128, 16)); st_rstd = sbuf("st_rstd", (128, 16))
    st_b = sbuf("st_b", (128, 16))
    ln_rstd = sbuf("ln_rstd", (128, NT)); ln_b2 = sbuf("ln_b2", (128, NT))
    tsh = sbuf("tsh", (128, 1024))
    s_eps = sbuf("s_eps", (128, 2))

    ps0 = psum("ps0", (128, 1024))
    ps1 = psum("ps1", (128, 1024))
    ps2 = psum("ps2", (128, 1024))
    psT = psum("psT", (128, 1024), BF16)
    psS = psum("psS", (128, 512))

    class Sem:
        __slots__ = ("h", "n")

        def __init__(self, name):
            self.h = ctx.enter_context(nc.semaphore(name))
            self.n = 0

        def inc(self, inst, k=1):
            inst.then_inc(self.h, k)
            self.n += k
            return self.n

        def dinc(self, inst):
            return self.inc(inst, 16)

        def wait(self, eng, n=None):
            eng.wait_ge(self.h, self.n if n is None else n)

    sQI = Sem("sQI"); sGI = Sem("sGI"); sV = Sem("sV"); sA = Sem("sA")
    sP = Sem("sP"); sW = Sem("sW"); sM = Sem("sM"); sO = Sem("sO")
    sX = [Sem("sX0"), Sem("sX1")]
    sWA = Sem("sWA"); sWB = Sem("sWB"); sWC = Sem("sWC")

    blk = ctx.enter_context(nc.Block())

    def body(eng):
        return getattr(blk, eng)

    def wview(h):
        return h.ap().rearrange("(cc p) n -> p cc n", p=128)

    def w3(dst):
        return dst[:].rearrange("p (cc n) -> p cc n", cc=FC)

    marks = {}

    AR.reset(0)
    a_lng = AR.bf16(1024)
    a_lnb = AR.f32(1024)
    PH_TOP = AR.top

    @body("sync")
    def _(y):
        sQI.dinc(y.dma_start(
            out=stream[:].rearrange("p (t f) -> p t f", t=NT),
            in_=dt["xb"].ap()[0:TL, :].rearrange("(t p) f -> p t f", p=128)))
        for src, dst in ((c_idb, s_idb), (c_msk, s_msk), (c_onesb, s_ones),
                         (c_onesf, s_onesf), (c_onescol, s_onescol)):
            sQI.dinc(y.dma_start(out=dst[:], in_=src))
        sQI.dinc(y.dma_start(out=s_prm[:], in_=dt["prm"].ap()))
        sQI.dinc(y.dma_start(out=s_um[:], in_=dt["umcol"].ap()))
        sQI.dinc(y.dma_start(out=s_bs1[:], in_=dt["bs1c"].ap()))
        sQI.dinc(y.dma_start(out=s_be1[:], in_=dt["be1c"].ap()))
        for nm in ("tmx", "tmw", "tmk", "tmv", "tmr", "tmg", "tdc"):
            sQI.dinc(y.dma_start(out=s_tm[nm][:], in_=dt[nm].ap()))
        sQI.dinc(y.dma_start(out=s_ucol[:], in_=dt["ucol"].ap()))
        sQI.dinc(y.dma_start(out=a_lnb[0:1, 0:1024], in_=dt["lnbr"].ap()))

    @body("gpsimd")
    def _(g):
        for src, dst in (("wqT", s_wqT), ("wkT", s_wkT), ("wvT", s_wvT)):
            sGI.dinc(g.dma_start(out=w3(dst), in_=wview(dt[src])))
        sGI.dinc(g.dma_start(out=a_lng[0:1, 0:1024], in_=dt["lngr"].ap()))
        sGI.dinc(g.dma_start(
            out=s_wrf[:].rearrange("p (c n) -> p c n", c=FC),
            in_=dt["wrf"].ap().rearrange("(cc p) n -> p cc n", p=128)))
        sGI.dinc(g.dma_start(out=s_brr[:], in_=dt["brr"].ap()))
        sGI.dinc(g.dma_start(out=s_bs2[:], in_=dt["bs2r"].ap()))
        sGI.dinc(g.dma_start(out=s_be2[:], in_=dt["be2"].ap()))

    @body("tensor")
    def _(t):
        sQI.wait(t)
        sGI.wait(t)
        for i in range(2):
            t.matmul(ps1[:, i * 512:(i + 1) * 512], s_ones[0:1, :],
                     a_lng[0:1, i * 512:(i + 1) * 512], start=True, stop=True)
        for i in range(2):
            li = t.matmul(ps0[:, i * 512:(i + 1) * 512], s_onesf[0:1, :],
                          a_lnb[0:1, i * 512:(i + 1) * 512], start=True, stop=True)
        sP.inc(li)

    @body("vector")
    def _(v):
        v.memset(s_eps[:, 0:1], EPS_LN)
        v.memset(s_eps[:, 1:2], EPS_GN)
        sP.wait(v)
        v.tensor_copy(out=a_lng, in_=ps1[:])
        sV.inc(v.tensor_copy(out=a_lnb, in_=ps0[:]))
    marks["lnbc"] = (sV.n, sP.n)

    RG = {}

    @body("gpsimd")
    def _(g):
        sQI.wait(g)
        for nm in ("rp", "rneed", "rnonce", "rf", "rpred", "roff", "rq"):
            RG[nm] = g.alloc_register(nm)
        g.load(RG["rp"], s_prm[0:1, 0:1])
        g.load(RG["rneed"], s_prm[0:1, 1:2])
        g.load(RG["rnonce"], s_prm[0:1, 2:3])
        g.store(s_flag[0:1, 0:1], RG["rnonce"])

    def ln_chain(n, eps):
        @body("vector")
        def _(v):
            sV.wait(v)
            sA.wait(v)
            v.tensor_scalar_mul(out=st_mean[:, 0:n], in0=st_sum[:, 0:n],
                                scalar1=1.0 / C)
            sV.inc(v.tensor_scalar_mul(out=st_rstd[:, 0:n], in0=st_sq[:, 0:n],
                                       scalar1=1.0 / C))
            sV.wait(v)
            sV.inc(v.tensor_tensor(out=st_b[:, 0:n], in0=st_mean[:, 0:n],
                                   in1=st_mean[:, 0:n], op=ALU.mult))
            sV.wait(v)
            sV.inc(v.tensor_sub(out=st_rstd[:, 0:n], in0=st_rstd[:, 0:n],
                                in1=st_b[:, 0:n]))

        @body("scalar")
        def _(a):
            sV.wait(a)
            sA.inc(a.activation(out=st_rstd[:, 0:n], in_=st_rstd[:, 0:n],
                                func=AF.Sqrt, bias=s_eps[:, eps:eps + 1]))

        @body("vector")
        def _(v):
            sA.wait(v)
            sV.inc(v.reciprocal(out=st_rstd[:, 0:n], in_=st_rstd[:, 0:n]))
            sV.wait(v)
            sV.inc(v.tensor_tensor(out=st_b[:, 0:n], in0=st_mean[:, 0:n],
                                   in1=st_rstd[:, 0:n], op=ALU.mult))
            sV.wait(v)
            sV.inc(v.tensor_scalar_mul(out=st_b[:, 0:n], in0=st_b[:, 0:n],
                                       scalar1=-1.0))

    AR.reset(PH_TOP)
    a_xlnb = AR.bf16(1024)
    a_qT = AR.bf16(4096, parts=64)
    a_kT = AR.bf16(4096, parts=64)
    a_vab = AR.bf16(NBT * HD)
    a_sc = AR.f32(4096)
    _pr = AR.bf16(4096)
    a_pr = [_pr, _pr]
    a_x4 = [AR.f32(1024) for _ in range(4)]
    a_xaT = AR.bf16(1024)
    qTv = a_qT
    kTv = a_kT

    def emit_attn_a():
        for grp in range(NBT // 4):
            tiles = [grp * 4 + i for i in range(4)]

            @body("sync")
            def _(y, grp=grp, tiles=tiles):
                if grp > 0:
                    sA.wait(y, marks[("a1xln", grp - 1)])
                for i, bt in enumerate(tiles):
                    sX[grp % 2].dinc(y.dma_start(
                        out=a_x4[i],
                        in_=dt["xb"].ap()[bt * 128:(bt + 1) * 128, :]))
            marks[("a1ld", grp)] = sX[grp % 2].n

            @body("vector")
            def _(v, grp=grp):
                sX[grp % 2].wait(v, marks[("a1ld", grp)])
                for i in range(4):
                    last = v.reduce_sum(out=st_sum[:, i:i + 1], in_=a_x4[i],
                                        axis=AX.X)
                sV.inc(last)

            @body("scalar")
            def _(a, grp=grp):
                sX[grp % 2].wait(a, marks[("a1ld", grp)])
                for i in range(4):
                    sA.inc(a.activation(out=tsh[:], in_=a_x4[i], func=AF.Square,
                                        accum_out=st_sq[:, i:i + 1]))
                    if i < 3:
                        sA.wait(a)

            ln_chain(4, 0)

            for i, bt in enumerate(tiles):
                @body("scalar")
                def _(a, i=i, bt=bt):
                    sV.wait(a)
                    if bt > 0:
                        sP.wait(a, marks[("a1tT", bt - 1)])
                    sA.inc(a.activation(out=a_xlnb, in_=a_x4[i],
                                        func=AF.Identity,
                                        bias=st_b[:, i:i + 1],
                                        scale=st_rstd[:, i:i + 1]))
                marks[("a1xlnb", bt)] = sA.n

                @body("tensor")
                def _(t, i=i, bt=bt):
                    sA.wait(t, marks[("a1xlnb", bt)])
                    if bt > 0:
                        sV.wait(t, marks[("a1xaTe", bt - 1)])
                    for cc in range(FC):
                        li = t.transpose(psT[:, cc * 128:(cc + 1) * 128],
                                         a_xlnb[:, cc * 128:(cc + 1) * 128],
                                         s_idb[:])
                    sP.inc(li)
                marks[("a1tT", bt)] = sP.n

                @body("vector")
                def _(v, i=i, bt=bt):
                    sP.wait(v, marks[("a1tT", bt)])
                    if bt > 0:
                        sP.wait(v, marks[("a1qkv", bt - 1)])
                    sV.inc(v.tensor_copy(out=a_xaT, in_=psT[:]))
                marks[("a1xaTe", bt)] = sV.n

                @body("tensor")
                def _(t, i=i, bt=bt):
                    sV.wait(t, marks[("a1xaTe", bt)])
                    if bt > 0:
                        sV.wait(t, marks[("a1qkve", bt - 1)])
                    for cc in range(FC):
                        t.matmul(psS[0:64, 0:128],
                                 s_wqT[:, cc * HD:(cc + 1) * HD],
                                 a_xaT[:, cc * 128:(cc + 1) * 128],
                                 start=(cc == 0), stop=(cc == FC - 1))
                    for cc in range(FC):
                        t.matmul(psS[0:64, 128:256],
                                 s_wkT[:, cc * HD:(cc + 1) * HD],
                                 a_xaT[:, cc * 128:(cc + 1) * 128],
                                 start=(cc == 0), stop=(cc == FC - 1))
                    for cc in range(FC):
                        li = t.matmul(ps2[:, 0:64],
                                      a_xaT[:, cc * 128:(cc + 1) * 128],
                                      s_wvT[:, cc * HD:(cc + 1) * HD],
                                      start=(cc == 0), stop=(cc == FC - 1))
                    sP.inc(li)
                marks[("a1qkv", bt)] = sP.n

                @body("vector")
                def _(v, i=i, bt=bt):
                    sP.wait(v, marks[("a1qkv", bt)])
                    v.tensor_copy(out=qTv[:, bt * 128:(bt + 1) * 128],
                                  in_=psS[0:64, 0:128])
                    v.tensor_copy(out=kTv[:, bt * 128:(bt + 1) * 128],
                                  in_=psS[0:64, 128:256])
                    sV.inc(v.tensor_copy(out=a_vab[:, bt * HD:(bt + 1) * HD],
                                         in_=ps2[:, 0:64]))
                marks[("a1qkve", bt)] = sV.n
            marks[("a1xln", grp)] = marks[("a1xlnb", tiles[-1])]

        for kt in range(NBT):
            for qg in range(8):
                gi = kt * 8 + qg
                half = qg % 2

                @body("tensor")
                def _(t, kt=kt, qg=qg, gi=gi, half=half):
                    if gi == 0:
                        sV.wait(t)
                    if gi >= 2:
                        sA.wait(t, marks[("a2sce", gi - 2)])
                    sP.inc(t.matmul(ps0[:, half * 512:half * 512 + 512],
                                    kTv[:, kt * 128:(kt + 1) * 128],
                                    qTv[:, qg * 512:(qg + 1) * 512],
                                    start=True, stop=True))
                marks[("a2scm", gi)] = sP.n

                @body("scalar")
                def _(a, kt=kt, qg=qg, gi=gi, half=half):
                    sP.wait(a, marks[("a2scm", gi)])
                    if kt > 0 and qg == 0:
                        sA.wait(a, marks[("a2exp", kt - 1)])
                    sA.inc(a.activation(out=a_sc[:, qg * 512:(qg + 1) * 512],
                                        in_=ps0[:, half * 512:half * 512 + 512],
                                        func=AF.Identity))
                marks[("a2sce", gi)] = sA.n

            @body("vector")
            def _(v, kt=kt):
                sA.wait(v, marks[("a2sce", kt * 8 + 7)])
                sV.inc(v.reduce_max(out=st_mean[:, 8:9], in_=a_sc, axis=AX.X,
                                    negate=True))
            marks[("a2max", kt)] = sV.n

            @body("scalar")
            def _(a, kt=kt):
                sV.wait(a, marks[("a2max", kt)])
                if kt >= 1:
                    sP.wait(a, marks[("a2avm", kt - 1)])
                sA.inc(a.activation(out=a_pr[kt % 2], in_=a_sc, func=AF.Exp,
                                    bias=st_mean[:, 8:9],
                                    accum_out=st_mean[:, 9:10]))
            marks[("a2exp", kt)] = sA.n

            @body("vector")
            def _(v, kt=kt):
                sA.wait(v, marks[("a2exp", kt)])
                sV.inc(v.reciprocal(out=st_mean[:, 10:11], in_=st_mean[:, 9:10]))
                sV.wait(v)
                sV.inc(v.tensor_scalar_mul(out=a_pr[kt % 2], in0=a_pr[kt % 2],
                                           scalar1=st_mean[:, 10:11]))
            marks[("a2nrm", kt)] = sV.n

            @body("tensor")
            def _(t, kt=kt):
                sV.wait(t, marks[("a2nrm", kt)])
                for qt in range(NT):
                    li = t.matmul(ps2[:, qt * 64:(qt + 1) * 64],
                                  a_pr[kt % 2][:, qt * 128:(qt + 1) * 128],
                                  a_vab[:, kt * HD:(kt + 1) * HD],
                                  start=(kt == 0 and qt % 8 == 0),
                                  stop=(kt == NBT - 1 and qt % 8 == 7))
                sP.inc(li)
            marks[("a2avm", kt)] = sP.n

        for qt in range(NT):
            @body("vector")
            def _(v, qt=qt):
                if qt == 0:
                    sP.wait(v)
                    sQI.wait(v)
                dst = stream[:, qt * 1024:(qt + 1) * 1024].rearrange(
                    "p (r n) -> p r n", r=NH)
                sV.inc(v.tensor_tensor(
                    out=dst, in0=dst,
                    in1=ps2[:, qt * 64:(qt + 1) * 64].unsqueeze(1)
                        .broadcast_to((128, NH, 64)),
                    op=ALU.add))
        marks["p1_done"] = sV.n

    if NSUB >= 1:
        emit_attn_a()
    marks["p1_psdone"] = sP.n
    marks[("res_prev",)] = marks.get("p1_done", 0)

    XW = FSPAN + 1

    def xtcol(cc, j0, n):
        return xT[:, cc * XW + 1 + j0: cc * XW + 1 + j0 + n]

    def ln_transpose_span(tag, tile0, ntile, xlnb_ap, save_stats=True):
        @body("vector")
        def _(v):
            sV.wait(v)
            for i in range(ntile):
                src = stream[:, (tile0 + i) * 1024:(tile0 + i + 1) * 1024]
                last = v.reduce_sum(out=st_sum[:, i:i + 1], in_=src, axis=AX.X)
            sV.inc(last)

        @body("scalar")
        def _(a):
            sA.wait(a)
            sV.wait(a, marks.get(("res_done", tag), 0))
            sM.wait(a)
            for i in range(ntile):
                src = stream[:, (tile0 + i) * 1024:(tile0 + i + 1) * 1024]
                sA.inc(a.activation(out=tsh[:], in_=src, func=AF.Square,
                                    accum_out=st_sq[:, i:i + 1]))
                if i < ntile - 1:
                    sA.wait(a)
        ln_chain(ntile, 0)
        if save_stats:
            @body("vector")
            def _(v):
                sV.wait(v)
                v.tensor_copy(out=ln_rstd[:, tile0:tile0 + ntile],
                              in_=st_rstd[:, 0:ntile])
                sV.inc(v.tensor_copy(out=ln_b2[:, tile0:tile0 + ntile],
                                     in_=st_b[:, 0:ntile]))
        for i in range(ntile):
            @body("scalar")
            def _(a, i=i):
                sV.wait(a)
                if i > 0:
                    sP.wait(a, marks[(tag, "lnT", i - 1)])
                sA.inc(a.activation(out=xlnb_ap,
                                    in_=stream[:, (tile0 + i) * 1024:
                                               (tile0 + i + 1) * 1024],
                                    func=AF.Identity, bias=st_b[:, i:i + 1],
                                    scale=st_rstd[:, i:i + 1]))
            marks[(tag, "lnx", i)] = sA.n

            @body("tensor")
            def _(t, i=i):
                sA.wait(t, marks[(tag, "lnx", i)])
                if i == 0:
                    sV.wait(t)
                else:
                    sV.wait(t, marks[(tag, "lne", i - 1)])
                for cc in range(FC):
                    li = t.transpose(psT[:, cc * 128:(cc + 1) * 128],
                                     xlnb_ap[:, cc * 128:(cc + 1) * 128],
                                     s_idb[:])
                sP.inc(li)
            marks[(tag, "lnT", i)] = sP.n

            @body("vector")
            def _(v, i=i):
                sP.wait(v, marks[(tag, "lnT", i)])
                dst = xT[:].rearrange("p (c w) -> p c w", c=FC)[
                    :, :, 1 + i * 128: 1 + (i + 1) * 128]
                sV.inc(v.tensor_copy(
                    out=dst, in_=psT[:].rearrange("p (c w) -> p c w", c=FC)))
            marks[(tag, "lne", i)] = sV.n
        marks[(tag, "ln_done")] = sV.n

    def emit_ffn(app):
        AR.reset(PH_TOP)
        a_xlnb = AR.bf16(1024)
        a_acc = AR.f32(8 * 1024)
        a_ghb = AR.bf16(FC * 1024)
        a_w2 = wB[:]
        a_hsb = AR.bf16(2 * 1024)
        a_gbc = AR.bf16(1024)
        a_gtm = AR.f32(8 * 64)
        a_e1 = AR.f32(8 * 32)
        a_tt = AR.f32(8 * 32)
        a_gates = AR.bf16(8 * 32)
        a_gfm = AR.bf16(1024, parts=32)
        a_growe = [AR.bf16(1024, parts=1), AR.bf16(1024, parts=1)]
        assert AR.top <= AR.n, ("ffn arena", AR.top)

        accslots = [(ps1, 0), (ps1, 512), (ps2, 0), (ps2, 512)]
        sRG = [Sem(f"sRG{app}0"), Sem(f"sRG{app}1")]

        for sp in range(2):
            tag = ("f", app, sp)
            tile0 = sp * 8
            marks[("res_done", tag)] = marks.get(("res_prev",), 0)
            ln_transpose_span(tag, tile0, 8, a_xlnb)

            @body("tensor")
            def _(t, tag=tag):
                sV.wait(t, marks[(tag, "ln_done")])
                for i in range(8):
                    for cc in range(FC):
                        t.matmul(psS[:, i * 64:i * 64 + NE],
                                 xtcol(cc, i * 128, 128),
                                 s_wrf[:, cc * NE:(cc + 1) * NE],
                                 start=(cc == 0), stop=False)
                    li = t.matmul(psS[:, i * 64:i * 64 + NE], s_ones[0:1, :],
                                  s_brr[0:1, :], start=False, stop=True)
                sP.inc(li)
            marks[(tag, "rtm")] = sP.n

            @body("vector")
            def _(v, tag=tag):
                sP.wait(v, marks[(tag, "rtm")])
                sV.inc(v.tensor_copy(
                    out=a_gtm.rearrange("p (i e) -> p i e", i=8)[:, :, 0:NE],
                    in_=psS[:, 0:512].rearrange("p (i e) -> p i e", i=8)
                        [:, :, 0:NE]))
                sV.wait(v)
                g3 = a_gtm.rearrange("p (i e) -> p i e", i=8)[:, :, 0:NE]
                sV.inc(v.reduce_max(out=st_mean[:, 0:8], in_=g3, axis=AX.X,
                                    negate=True))
            marks[(tag, "rmax")] = sV.n

            @body("scalar")
            def _(a, tag=tag):
                sV.wait(a, marks[(tag, "rmax")])
                for i in range(8):
                    li = a.activation(
                        out=a_gtm[:, i * 64:i * 64 + NE],
                        in_=a_gtm[:, i * 64:i * 64 + NE], func=AF.Exp,
                        bias=st_mean[:, i:i + 1],
                        accum_out=st_sq[:, i:i + 1])
                sA.inc(li)
            marks[(tag, "rexp")] = sA.n

            @body("vector")
            def _(v, tag=tag):
                g3 = a_gtm.rearrange("p (i e) -> p i e", i=8)[:, :, 0:NE]
                t3 = a_tt.rearrange("p (i e) -> p i e", i=8)[:, :, 0:NE]
                sA.wait(v, marks[(tag, "rexp")])
                sV.inc(v.reciprocal(out=st_sq[:, 0:8], in_=st_sq[:, 0:8]))
                sV.wait(v)
                for i in range(8):
                    li = v.tensor_scalar_mul(out=a_gtm[:, i * 64:i * 64 + NE],
                                             in0=a_gtm[:, i * 64:i * 64 + NE],
                                             scalar1=st_sq[:, i:i + 1])
                sV.inc(li)
                sV.wait(v)
                sV.inc(v.reduce_max(out=st_mean[:, 0:8], in_=g3, axis=AX.X))
                sV.wait(v)
                for i in range(8):
                    li = v.tensor_scalar(out=a_e1[:, i * 32:i * 32 + NE],
                                         in0=a_gtm[:, i * 64:i * 64 + NE],
                                         scalar1=st_mean[:, i:i + 1],
                                         scalar2=None, op0=ALU.is_equal)
                sV.inc(li)
                sV.wait(v)
                for i in range(8):
                    li = v.scalar_tensor_tensor(
                        out=a_tt[:, i * 32:i * 32 + NE],
                        in0=a_e1[:, i * 32:i * 32 + NE], scalar=-1e9,
                        in1=a_gtm[:, i * 64:i * 64 + NE],
                        op0=ALU.mult, op1=ALU.add)
                sV.inc(li)
                sV.wait(v)
                sV.inc(v.reduce_max(out=st_mean[:, 0:8], in_=t3, axis=AX.X))
                sV.wait(v)
                for i in range(8):
                    li = v.tensor_scalar(out=a_e1[:, i * 32:i * 32 + NE],
                                         in0=a_tt[:, i * 32:i * 32 + NE],
                                         scalar1=st_mean[:, i:i + 1],
                                         scalar2=None, op0=ALU.is_equal)
                sV.inc(li)
                sV.wait(v)
                for i in range(8):
                    li = v.scalar_tensor_tensor(
                        out=a_tt[:, i * 32:i * 32 + NE],
                        in0=a_e1[:, i * 32:i * 32 + NE], scalar=-1e9,
                        in1=a_tt[:, i * 32:i * 32 + NE],
                        op0=ALU.mult, op1=ALU.add)
                sV.inc(li)
                sV.wait(v)
                sV.inc(v.reduce_max(out=st_mean[:, 0:8], in_=t3, axis=AX.X))
                sV.wait(v)
                for i in range(8):
                    li = v.tensor_scalar(out=a_e1[:, i * 32:i * 32 + NE],
                                         in0=a_gtm[:, i * 64:i * 64 + NE],
                                         scalar1=st_mean[:, i:i + 1],
                                         scalar2=None, op0=ALU.is_ge)
                sV.inc(li)
                sV.inc(v.memset(a_gates, 0.0))
                sV.wait(v)
                for i in range(8):
                    li = v.tensor_tensor(out=a_gates[:, i * 32:i * 32 + NE],
                                         in0=a_gtm[:, i * 64:i * 64 + NE],
                                         in1=a_e1[:, i * 32:i * 32 + NE],
                                         op=ALU.mult)
                sV.inc(li)
            marks[(tag, "gates")] = sV.n

            @body("tensor")
            def _(t, tag=tag):
                sV.wait(t, marks[(tag, "gates")])
                sV.wait(t, marks[(tag, "lne", 7)])
                for i in range(8):
                    li = t.transpose(psT[0:32, i * 128:(i + 1) * 128],
                                     a_gates[:, i * 32:(i + 1) * 32], s_idb[:])
                sP.inc(li)
            marks[(tag, "gfmT")] = sP.n

            @body("vector")
            def _(v, tag=tag):
                sP.wait(v, marks[(tag, "gfmT")])
                sV.inc(v.tensor_copy(
                    out=a_gfm.rearrange("p (i w) -> p i w", i=8),
                    in_=psT[0:32, :].rearrange("p (i w) -> p i w", i=8)))
            marks[(tag, "gfm")] = sV.n

            acc_hist = []
            for i in range(8):
                for dmh in range(2):
                    sl = accslots[(i * 2 + dmh) % 4]

                    @body("tensor")
                    def _(t, i=i, dmh=dmh, sl=sl, tag=tag):
                        if i == 0 and dmh == 0:
                            sV.wait(t, marks[(tag, "gfm")])
                        if len(acc_hist) >= 4:
                            sV.wait(t, marks[acc_hist[-4]])
                        t.matmul(sl[0][:, sl[1]:sl[1] + 512],
                                 a_gfm[0:NE, i * 128:(i + 1) * 128],
                                 s_be2[0:NE, dmh * 512:dmh * 512 + 512],
                                 start=True, stop=False)
                        li = t.matmul(sl[0][:, sl[1]:sl[1] + 512],
                                      s_ones[0:1, :],
                                      s_bs2[0:1, dmh * 512:dmh * 512 + 512],
                                      start=False, stop=True)
                        sP.inc(li)
                    marks[(tag, "ainitm", i, dmh)] = sP.n

                    @body("vector")
                    def _(v, i=i, dmh=dmh, sl=sl, tag=tag):
                        sP.wait(v, marks[(tag, "ainitm", i, dmh)])
                        sV.inc(v.tensor_copy(
                            out=a_acc[:, i * 1024 + dmh * 512:
                                      i * 1024 + dmh * 512 + 512],
                            in_=sl[0][:, sl[1]:sl[1] + 512]))
                    marks[(tag, "acck", i * 2 + dmh)] = sV.n
                    acc_hist.append((tag, "acck", i * 2 + dmh))

            for e in range(NE + 1):
                shared = (e == NE)
                sl1 = wA
                se1 = sWA
                w1src = (dt["ws1"].ap() if shared else dt["we1"].ap()[e])
                w2src = (dt["ws2"].ap() if shared else dt["we2"].ap()[e])

                @body("gpsimd")
                def _(g, e=e, tag=tag, w1src=w1src, w2src=w2src, sl1=sl1,
                      se1=se1):
                    sP.wait(g, marks.get((tag, "hm", e - 1, FC - 1, 1),
                                         marks.get(("w1free_prev",),
                                                   marks.get("p1_psdone", 0))))
                    se1.dinc(g.dma_start(
                        out=sl1[:].rearrange("p (c n) -> p c n", c=FC),
                        in_=w1src.rearrange("(cc p) n -> p cc n", p=128)))
                    sP.wait(g, marks.get((tag, "w2g", e - 1, 15),
                                         marks.get(("w2free_prev",),
                                                   marks.get("p1_psdone", 0))))
                    sWC.dinc(g.dma_start(
                        out=a_w2.rearrange("p (c n) -> p c n", c=FC),
                        in_=w2src.rearrange("(ic p) n -> p ic n", p=128)))
                marks[(tag, "w1ld", e)] = se1.n
                marks[(tag, "w2ld", e)] = sWC.n

                if not shared:
                    @body("sync")
                    def _(y, e=e, tag=tag):
                        sV.wait(y, marks[(tag, "gfm")])
                        if e >= 2:
                            sP.wait(y, marks[(tag, "gbcm", e - 2, 1)])
                        sRG[e % 2].dinc(y.dma_start(
                            out=a_growe[e % 2],
                            in_=a_gfm[e:e + 1, :]))
                    marks[(tag, "grow", e)] = sRG[e % 2].n

                    for tg in range(2):
                        @body("tensor")
                        def _(t, e=e, tg=tg, tag=tag):
                            sRG[e % 2].wait(t, marks[(tag, "grow", e)])
                            if tg == 0 and e > 0:
                                sV.wait(t, marks[(tag, "gbce", e - 1, 1)])
                            if tg == 1:
                                sV.wait(t, marks[(tag, "gbce", e, 0)])
                            sP.inc(t.matmul(
                                psS[:, 0:512], s_ones[0:1, :],
                                a_growe[e % 2][0:1, tg * 512:(tg + 1) * 512],
                                start=True, stop=True))
                        marks[(tag, "gbcm", e, tg)] = sP.n

                        @body("vector")
                        def _(v, e=e, tg=tg, tag=tag):
                            sP.wait(v, marks[(tag, "gbcm", e, tg)])
                            sV.inc(v.tensor_copy(
                                out=a_gbc[:, tg * 512:(tg + 1) * 512],
                                in_=psS[:, 0:512]))
                        marks[(tag, "gbce", e, tg)] = sV.n

                for ic in range(FC):
                    for tg in range(2):
                        @body("tensor")
                        def _(t, e=e, ic=ic, tg=tg, tag=tag, sl1=sl1, se1=se1):
                            if ic == 0 and tg == 0:
                                se1.wait(t, marks[(tag, "w1ld", e)])
                                if e == 0:
                                    sV.wait(t, marks[acc_hist[-1]])
                            if ic > 0:
                                sA.wait(t, marks[(tag, "gelu", e, ic - 1, tg)])
                            for cc in range(FC):
                                li = t.matmul(
                                    ps0[:, tg * 512:tg * 512 + 512],
                                    sl1[:, cc * 1024 + ic * 128:
                                        cc * 1024 + ic * 128 + 128],
                                    xtcol(cc, tg * 512, 512),
                                    start=(cc == 0), stop=(cc == FC - 1))
                            sP.inc(li)
                        marks[(tag, "hm", e, ic, tg)] = sP.n

                        @body("scalar")
                        def _(a, e=e, ic=ic, tg=tg, tag=tag, shared=shared):
                            sP.wait(a, marks[(tag, "hm", e, ic, tg)])
                            if shared:
                                dst = a_ghb[:, ic * 1024 + tg * 512:
                                            ic * 1024 + tg * 512 + 512]
                                if e > 0 and ic == 0:
                                    sP.wait(a, marks[(tag, "w2m", e - 1)])
                            else:
                                dst = a_hsb[:, (ic % 2) * 1024 + tg * 512:
                                            (ic % 2) * 1024 + tg * 512 + 512]
                                if ic >= 2:
                                    sV.wait(a, marks[(tag, "gh", e, ic - 2, tg)])
                            bias = (s_bs1[:, ic:ic + 1] if shared
                                    else s_be1[:, e * FC + ic:e * FC + ic + 1])
                            sA.inc(a.activation(
                                out=dst, in_=ps0[:, tg * 512:tg * 512 + 512],
                                func=AF.Gelu, bias=bias))
                        marks[(tag, "gelu", e, ic, tg)] = sA.n

                        if not shared:
                            @body("vector")
                            def _(v, e=e, ic=ic, tg=tg, tag=tag):
                                sA.wait(v, marks[(tag, "gelu", e, ic, tg)])
                                sV.wait(v, marks[(tag, "gbce", e, tg)])
                                if e > 0 and ic == 0:
                                    sP.wait(v, marks[(tag, "w2m", e - 1)])
                                sV.inc(v.tensor_tensor(
                                    out=a_ghb[:, ic * 1024 + tg * 512:
                                              ic * 1024 + tg * 512 + 512],
                                    in0=a_hsb[:, (ic % 2) * 1024 + tg * 512:
                                              (ic % 2) * 1024 + tg * 512 + 512],
                                    in1=a_gbc[:, tg * 512:(tg + 1) * 512],
                                    op=ALU.mult))
                            marks[(tag, "gh", e, ic, tg)] = sV.n
                        else:
                            marks[(tag, "gh", e, ic, tg)] = \
                                marks[(tag, "gelu", e, ic, tg)]

                ghwait = max(marks[(tag, "gh", e, icq, tgq)]
                             for icq in range(FC) for tgq in range(2))
                gelwait = max(marks[(tag, "gelu", e, icq, tgq)]
                              for icq in range(FC) for tgq in range(2))
                for tt in range(8):
                    for dmh in range(2):
                        sl = accslots[(tt * 2 + dmh) % 4]

                        @body("tensor")
                        def _(t, e=e, tt=tt, dmh=dmh, tag=tag, sl=sl,
                              shared=shared):
                            if tt == 0 and dmh == 0:
                                sWC.wait(t, marks[(tag, "w2ld", e)])
                                if shared:
                                    sA.wait(t, gelwait)
                                else:
                                    sV.wait(t, ghwait)
                            sV.wait(t, marks[acc_hist[-4]])
                            for ic in range(FC):
                                li = t.matmul(
                                    sl[0][:, sl[1]:sl[1] + 512],
                                    a_ghb[:, ic * 1024 + tt * 128:
                                          ic * 1024 + tt * 128 + 128],
                                    a_w2[:, ic * 1024 + dmh * 512:
                                         ic * 1024 + dmh * 512 + 512],
                                    start=(ic == 0), stop=(ic == FC - 1))
                            sP.inc(li)
                        marks[(tag, "w2g", e, tt * 2 + dmh)] = sP.n

                        @body("vector")
                        def _(v, e=e, tt=tt, dmh=dmh, tag=tag, sl=sl):
                            sP.wait(v, marks[(tag, "w2g", e, tt * 2 + dmh)])
                            dst = a_acc[:, tt * 1024 + dmh * 512:
                                        tt * 1024 + dmh * 512 + 512]
                            sV.inc(v.tensor_tensor(
                                out=dst, in0=dst,
                                in1=sl[0][:, sl[1]:sl[1] + 512], op=ALU.add))
                        marks[(tag, "acck2", e, tt * 2 + dmh)] = sV.n
                        acc_hist.append((tag, "acck2", e, tt * 2 + dmh))
                marks[(tag, "w2m", e)] = sP.n

            marks[("w1free_prev",)] = marks[(tag, "hm", NE, FC - 1, 1)]
            marks[("w2free_prev",)] = marks[(tag, "w2g", NE, 15)]

            lastacc = max(marks[k] for k in acc_hist[-16:])
            for i in range(8):
                @body("vector")
                def _(v, i=i, tag=tag, tile0=tile0, lastacc=lastacc):
                    sV.wait(v)
                    if i == 0 and app == 0 and sp == 0:
                        sQI.wait(v)
                    tile = tile0 + i
                    strm = stream[:, tile * 1024:(tile + 1) * 1024]
                    sV.inc(v.tensor_scalar(out=tsh[:], in0=strm,
                                           scalar1=ln_rstd[:, tile:tile + 1],
                                           scalar2=ln_b2[:, tile:tile + 1],
                                           op0=ALU.mult, op1=ALU.add))
                    sV.wait(v)
                    sV.inc(v.tensor_tensor(out=tsh[:], in0=tsh[:],
                                           in1=a_acc[:, i * 1024:(i + 1) * 1024],
                                           op=ALU.add))
                    sV.wait(v)
                    sV.inc(v.tensor_tensor(out=strm, in0=strm, in1=tsh[:],
                                           op=ALU.add))
                marks[(tag, "res", i)] = sV.n
            marks[("res_prev",)] = sV.n

    if NSUB >= 2:
        emit_ffn(0)

    def emit_attn_b():
        AR.reset(PH_TOP)
        a_xlnb = AR.bf16(1024)
        a_w = AR.f32(FC * BSPAN)
        a_rT = AR.bf16(FC * BSPAN)
        a_g = a_rT
        a_kTp = AR.bf16(FC * BSPAN)
        a_v = AR.bf16(2 * 1024)
        a_D = AR.bf16(FC * BSPAN)
        a_xmix = AR.bf16(FC * BSPAN)
        a_yg = a_xmix
        a_mj = AR.bf16(FC * BSPAN)
        a_ygT = a_mj
        a_mf = AR.bf16(5 * BSPAN, parts=32)
        a_tdh = AR.bf16(BSPAN, parts=64)
        a_sc2 = AR.bf16(BSPAN)
        a_csp = AR.f32(2 * CHK)
        a_expw = AR.f32(CHK)
        a_zero = AR.f32(CHK)
        a_ecsr = AR.bf16(2 * CHK)
        a_ecsk = AR.bf16(2 * CHK)
        a_ecsl = AR.f32(8)
        a_rt = AR.bf16(2 * CHK)
        a_kt = AR.bf16(2 * CHK)
        a_ruk = AR.bf16(CHK)
        a_A = AR.bf16(2 * 256)
        a_ktt = AR.bf16(128)
        a_st = AR.f32(64)
        a_sbf = AR.bf16(512)
        a_dvc = AR.f32(16)
        a_y = AR.f32(1024)
        a_yn = AR.bf16(2 * 1024)
        a_maa1 = AR.bf16(FC * 160)
        a_maa2 = AR.bf16(5 * C, parts=32)
        a_td1 = AR.bf16(FC * 64)
        a_td2 = AR.bf16(C, parts=64)
        a_lcrx = AR.bf16(FC)
        a_lc0 = AR.bf16(FC)
        a_rowb = AR.bf16(C, parts=1)
        assert AR.top <= AR.n, ("attn_b arena", AR.top)

        sG3 = Sem("sG3")
        NCH = BSPAN // CHK
        NT2 = BSPAN // 128

        def ln_chain_gn(n):
            @body("vector")
            def _(v):
                sV.wait(v)
                v.tensor_scalar_mul(out=st_mean[:, 0:n], in0=st_sum[:, 0:n],
                                    scalar1=1.0 / HK)
                sV.inc(v.tensor_scalar_mul(out=st_rstd[:, 0:n],
                                           in0=st_sq[:, 0:n], scalar1=1.0 / HK))
                sV.wait(v)
                sV.inc(v.tensor_tensor(out=st_b[:, 0:n], in0=st_mean[:, 0:n],
                                       in1=st_mean[:, 0:n], op=ALU.mult))
                sV.wait(v)
                sV.inc(v.tensor_sub(out=st_rstd[:, 0:n], in0=st_rstd[:, 0:n],
                                    in1=st_b[:, 0:n]))

            @body("scalar")
            def _(a):
                sV.wait(a)
                sA.inc(a.activation(out=st_rstd[:, 0:n], in_=st_rstd[:, 0:n],
                                    func=AF.Sqrt, bias=s_eps[:, 1:2]))

            @body("vector")
            def _(v):
                sA.wait(v)
                sV.inc(v.reciprocal(out=st_rstd[:, 0:n], in_=st_rstd[:, 0:n]))
                sV.wait(v)
                sV.inc(v.tensor_tensor(out=st_b[:, 0:n], in0=st_mean[:, 0:n],
                                       in1=st_rstd[:, 0:n], op=ALU.mult))
                sV.wait(v)
                sV.inc(v.tensor_scalar_mul(out=st_b[:, 0:n], in0=st_b[:, 0:n],
                                           scalar1=-1.0))

        @body("gpsimd")
        def _(g):
            sP.wait(g, marks[("w2free_prev",)])
            sG3.dinc(g.dma_start(out=a_maa1, in_=dt["maa_w1"].ap()))
            sG3.dinc(g.dma_start(out=a_maa2, in_=dt["maa_w2"].ap()))
            sG3.dinc(g.dma_start(out=a_td1, in_=dt["td_w1"].ap()))
            sG3.dinc(g.dma_start(out=a_td2, in_=dt["td_w2"].ap()))
        marks["p3c"] = sG3.n

        @body("vector")
        def _(v):
            sV.wait(v)
            sV.inc(v.reduce_sum(out=st_sum[:, 0:1],
                                in_=stream[:, 15 * 1024:16 * 1024], axis=AX.X))

        @body("scalar")
        def _(a):
            sA.wait(a)
            sV.wait(a, marks[("res_prev",)])
            sA.inc(a.activation(out=tsh[:], in_=stream[:, 15 * 1024:16 * 1024],
                                func=AF.Square, accum_out=st_sq[:, 0:1]))
        ln_chain(1, 0)

        @body("scalar")
        def _(a):
            sV.wait(a)
            sA.inc(a.activation(out=tsh[:], in_=stream[:, 15 * 1024:16 * 1024],
                                func=AF.Identity, bias=st_b[:, 0:1],
                                scale=st_rstd[:, 0:1]))
        marks["x1row"] = sA.n

        @body("vector")
        def _(v):
            sV.wait(v)
            v.memset(a_lc0, 0.0)
            sV.inc(v.memset(s_S[:], 0.0))
        marks["sinit0"] = sV.n

        @body("gpsimd")
        def _(g):
            sA.wait(g, marks["x1row"])
            if SIMSAFE:
                sM.inc(g.memset(a_rowb, 0.0))
                return
            g.reg_alu(RG["roff"], RG["rp"], C, ALU.mult)
            sM.dinc(g.dma_start(out=bass.AP(mail_x, RG["roff"],
                                            [[C, 1], [1, C]]),
                                in_=tsh[127:128, :]))
            sM.wait(g)
            g.reg_alu(RG["roff"], RG["rp"], 32, ALU.mult)
            sM.dinc(g.dma_start(out=bass.AP(mail_f, RG["roff"],
                                            [[16, 1], [1, 1]]),
                                in_=s_flag[0:1, 0:1]))
            sM.wait(g)
            g.reg_alu(RG["rq"], RG["rp"], 1, ALU.bitwise_xor)
            g.reg_alu(RG["roff"], RG["rq"], 32, ALU.mult)
            g.reg_mov(RG["rpred"], 1)
            with g.While(RG["rpred"]):
                g.load(RG["rf"], bass.AP(mail_f, RG["roff"], [[16, 1], [1, 1]]))
                g.reg_alu(RG["rpred"], RG["rf"], RG["rnonce"], ALU.is_equal)
                g.reg_alu(RG["rpred"], RG["rpred"], 1, ALU.bitwise_xor)
                g.reg_alu(RG["rpred"], RG["rpred"], RG["rneed"], ALU.bitwise_and)
            g.reg_alu(RG["roff"], RG["rq"], C, ALU.mult)
            sM.dinc(g.dma_start(
                out=a_rowb,
                in_=bass.AP(mail_x, RG["roff"], [[C, 1], [1, C]])))
            sM.wait(g)
        marks["x1rx"] = sM.n

        @body("tensor")
        def _(t):
            sM.wait(t, marks["x1rx"])
            sV.wait(t)
            sA.wait(t)
            for fc2 in range(FC):
                li = t.transpose(psT[:, 2 * fc2:2 * fc2 + 1],
                                 a_rowb[0:1, fc2 * 128:(fc2 + 1) * 128],
                                 s_idb[0:1, 0:1])
            sP.inc(li)
        marks["x1rT"] = sP.n

        @body("vector")
        def _(v):
            sP.wait(v, marks["x1rT"])
            sV.wait(v)
            sV.inc(v.tensor_copy(
                out=a_lcrx,
                in_=psT[:, 0:2 * FC].rearrange("p (c two) -> p c two", two=2)
                [:, :, 0:1].squeeze(2)))
        marks["x1rxe"] = sV.n

        @body("vector")
        def _(v):
            sV.wait(v, marks["x1rxe"])
            sV.inc(v.copy_predicated(
                out=a_lc0, mask=s_um[:, 0:1].broadcast_to((128, FC)),
                data=a_lcrx))
        marks["lc0"] = sV.n

        wmap = {"k": "wkbT", "v": "wvbT", "r": "wrT", "g": "wgT", "o": "woT"}
        fidx = {"w": 0, "k": 1, "v": 2, "r": 3, "g": 4}
        last_supd = {"v": marks["sinit0"]}

        def emit_span(pas, sp):
            tag = ("b", pas, sp)
            tile0 = sp * NT2
            marks[("res_done", tag)] = marks.get(("res_prev",), 0)
            ln_transpose_span(tag, tile0, NT2, a_xlnb)

            @body("vector")
            def _(v, tag=tag, sp=sp, pas=pas):
                sV.wait(v)
                x3 = xT[:].rearrange("p (c w) -> p c w", c=FC)
                if sp == 0:
                    sV.wait(v, marks["lc0"])
                    src = a_lc0
                else:
                    src = s_lastcol[:]
                sV.inc(v.tensor_copy(out=x3[:, :, 0:1], in_=src.unsqueeze(2)))
                sV.wait(v)
                sV.inc(v.tensor_copy(out=s_lastcol[:],
                                     in_=x3[:, :, BSPAN:BSPAN + 1].squeeze(2)))
            marks[(tag, "bnd")] = sV.n
            if BCUT <= 1:
                return

            @body("vector")
            def _(v, tag=tag, pas=pas, sp=sp):
                sV.wait(v, marks[(tag, "bnd")])
                sV.wait(v)
                if pas == 0 and sp == 0:
                    sG3.wait(v, marks["p3c"])
                for cc in range(FC):
                    li = v.tensor_sub(out=a_D[:, cc * BSPAN:(cc + 1) * BSPAN],
                                      in0=xT[:, cc * XW:cc * XW + BSPAN],
                                      in1=xT[:, cc * XW + 1:cc * XW + 1 + BSPAN])
                sV.inc(li)
                sV.wait(v)
                for cc in range(FC):
                    li = v.scalar_tensor_tensor(
                        out=a_xmix[:, cc * BSPAN:(cc + 1) * BSPAN],
                        in0=a_D[:, cc * BSPAN:(cc + 1) * BSPAN],
                        scalar=s_tm["tmx"][:, cc:cc + 1],
                        in1=xT[:, cc * XW + 1:cc * XW + 1 + BSPAN],
                        op0=ALU.mult, op1=ALU.add)
                sV.inc(li)
            marks[(tag, "z")] = sV.n
            if BCUT <= 2:
                return

            for f in range(5):
                @body("tensor")
                def _(t, f=f, tag=tag):
                    if f == 0:
                        sV.wait(t, marks[(tag, "z")])
                        sA.wait(t)
                        sV.wait(t)
                    psd, off = ((ps2, f * 256) if f < 4 else (psS, 0))
                    for cc in range(FC):
                        li = t.matmul(
                            psd[0:32, off:off + BSPAN],
                            a_maa1[:, cc * 160 + f * 32:cc * 160 + f * 32 + 32],
                            a_xmix[:, cc * BSPAN:(cc + 1) * BSPAN],
                            start=(cc == 0), stop=(cc == FC - 1))
                    sP.inc(li)
                marks[(tag, "mixm", f)] = sP.n

            @body("scalar")
            def _(a, tag=tag):
                sP.wait(a, marks[(tag, "mixm", 4)])
                sA.wait(a)
                for f in range(5):
                    psd, off = ((ps2, f * 256) if f < 4 else (psS, 0))
                    li = a.activation(out=a_mf[0:32, f * BSPAN:(f + 1) * BSPAN],
                                      in_=psd[0:32, off:off + BSPAN],
                                      func=AF.Tanh)
                sA.inc(li)
            marks[(tag, "mixt")] = sA.n
            if BCUT <= 3:
                return

            jlist = ["w", "k", "v"] if pas == 0 else ["w", "k", "v", "r"]
            rotseq = [jj for jj in jlist if jj != "w"]
            if pas == 1:
                rotseq += ["g", "o"]
            wrot = {}
            for idx, jj in enumerate(rotseq):
                wrot[jj] = ("A" if idx % 2 == 0 else "B",
                            sWA if idx % 2 == 0 else sWB)

            def emit_wload(jj, tag=tag):
                @body("gpsimd")
                def _(g, jj=jj, tag=tag):
                    wname, wsem = wrot[jj]
                    key = ("wslotfree", wname)
                    sP.wait(g, marks.get(key, marks.get(("w2free_prev",), 0)))
                    wsem.wait(g)
                    wsem.dinc(g.dma_start(
                        out=(wA if wname == "A" else wB)[:]
                        .rearrange("p (c n) -> p c n", c=FC),
                        in_=dt[wmap[jj]].ap()
                        .rearrange("(cc p) n -> p cc n", p=128)))
                marks[(tag, "wld", jj)] = wrot[jj][1].n

            for jj in rotseq[:2]:
                emit_wload(jj)
            pend = {"i": 2}

            def emit_mix(j, tag):
                f = fidx[j]
                for cc in range(FC):
                    @body("tensor")
                    def _(t, cc=cc, tag=tag, j=j, f=f):
                        if cc == 0:
                            sA.wait(t)
                            sV.wait(t)
                        else:
                            sV.wait(t, marks[(tag, "mje", j, cc - 1)])
                        sP.inc(t.matmul(
                            ps2[:, (cc % 2) * 512:(cc % 2) * 512 + BSPAN],
                            a_maa2[0:32, f * C + cc * 128:f * C + cc * 128 + 128],
                            a_mf[0:32, f * BSPAN:(f + 1) * BSPAN],
                            start=True, stop=True))
                    marks[(tag, "mjm", j, cc)] = sP.n

                    @body("vector")
                    def _(v, cc=cc, tag=tag, j=j):
                        sP.wait(v, marks[(tag, "mjm", j, cc)])
                        sV.inc(v.tensor_copy(
                            out=a_mj[:, cc * BSPAN:(cc + 1) * BSPAN],
                            in_=ps2[:, (cc % 2) * 512:(cc % 2) * 512 + BSPAN]))
                    marks[(tag, "mje", j, cc)] = sV.n

                @body("vector")
                def _(v, tag=tag, j=j):
                    sV.wait(v)
                    tmj = s_tm["tm" + j]
                    for cc in range(FC):
                        v.scalar_tensor_tensor(
                            out=a_xmix[:, cc * BSPAN:(cc + 1) * BSPAN],
                            in0=a_D[:, cc * BSPAN:(cc + 1) * BSPAN],
                            scalar=tmj[:, cc:cc + 1],
                            in1=xT[:, cc * XW + 1:cc * XW + 1 + BSPAN],
                            op0=ALU.mult, op1=ALU.add)
                        sV.inc(v.tensor_tensor(
                            out=a_sc2[:, 0:BSPAN],
                            in0=a_D[:, cc * BSPAN:(cc + 1) * BSPAN],
                            in1=a_mj[:, cc * BSPAN:(cc + 1) * BSPAN],
                            op=ALU.mult))
                        sV.wait(v)
                        sV.inc(v.tensor_add(
                            out=a_xmix[:, cc * BSPAN:(cc + 1) * BSPAN],
                            in0=a_xmix[:, cc * BSPAN:(cc + 1) * BSPAN],
                            in1=a_sc2[:, 0:BSPAN]))
                        if cc < FC - 1:
                            sV.wait(v)
                marks[(tag, "xmix", j)] = sV.n

            def emit_proj_w(tag):
                @body("tensor")
                def _(t, tag=tag):
                    sV.wait(t, marks[(tag, "xmix", "w")])
                    sA.wait(t)
                    for cc in range(FC):
                        li = t.matmul(psS[0:64, 0:BSPAN],
                                      a_td1[:, cc * 64:(cc + 1) * 64],
                                      a_xmix[:, cc * BSPAN:(cc + 1) * BSPAN],
                                      start=(cc == 0), stop=(cc == FC - 1))
                    sP.inc(li)
                marks[(tag, "tdm")] = sP.n

                @body("scalar")
                def _(a, tag=tag):
                    sP.wait(a, marks[(tag, "tdm")])
                    sA.inc(a.activation(out=a_tdh[0:64, 0:BSPAN],
                                        in_=psS[0:64, 0:BSPAN], func=AF.Tanh))
                marks[(tag, "tdt")] = sA.n

                for oc in range(FC):
                    @body("tensor")
                    def _(t, oc=oc, tag=tag):
                        if oc == 0:
                            sA.wait(t, marks[(tag, "tdt")])
                        else:
                            sA.wait(t, marks[(tag, "we", oc - 1)])
                        sP.inc(t.matmul(
                            ps2[:, (oc % 2) * 512:(oc % 2) * 512 + BSPAN],
                            a_td2[0:64, oc * 128:(oc + 1) * 128],
                            a_tdh[0:64, 0:BSPAN],
                            start=True, stop=True))
                    marks[(tag, "wm", oc)] = sP.n

                    @body("scalar")
                    def _(a, oc=oc, tag=tag):
                        sP.wait(a, marks[(tag, "wm", oc)])
                        sA.inc(a.activation(
                            out=a_w[:, oc * BSPAN:(oc + 1) * BSPAN],
                            in_=ps2[:, (oc % 2) * 512:(oc % 2) * 512 + BSPAN],
                            func=AF.Identity, bias=s_tm["tdc"][:, oc:oc + 1]))
                    marks[(tag, "we", oc)] = sA.n
                marks[(tag, "proj", "w")] = sA.n

            def emit_proj_fm(j, dst, tag):
                wname, wsem = wrot[j]
                wslot = wA if wname == "A" else wB
                for oc in range(FC):
                    @body("tensor")
                    def _(t, oc=oc, tag=tag, j=j, wslot=wslot, wsem=wsem):
                        if oc == 0:
                            sV.wait(t, marks[(tag, "xmix", j)])
                            wsem.wait(t, marks[(tag, "wld", j)])
                        else:
                            sV.wait(t, marks[(tag, "pje", j, oc - 1)])
                        for cc in range(FC):
                            li = t.matmul(
                                ps2[:, (oc % 2) * 512:(oc % 2) * 512 + BSPAN],
                                wslot[:, cc * 1024 + oc * 128:
                                      cc * 1024 + oc * 128 + 128],
                                a_xmix[:, cc * BSPAN:(cc + 1) * BSPAN],
                                start=(cc == 0), stop=(cc == FC - 1))
                        sP.inc(li)
                    marks[(tag, "pjm", j, oc)] = sP.n

                    @body("vector")
                    def _(v, oc=oc, tag=tag, j=j, dst=dst):
                        sP.wait(v, marks[(tag, "pjm", j, oc)])
                        sV.inc(v.tensor_copy(
                            out=dst[:, oc * BSPAN:(oc + 1) * BSPAN],
                            in_=ps2[:, (oc % 2) * 512:(oc % 2) * 512 + BSPAN]))
                    marks[(tag, "pje", j, oc)] = sV.n
                marks[(tag, "proj", j)] = sV.n
                marks[("wslotfree", wname)] = sP.n

            def emit_proj_tm(j, dst, tag):
                wname, wsem = wrot[j]
                wslot = wA if wname == "A" else wB
                for t2 in range(NT2):
                    for half in range(2):
                        idx = t2 * 2 + half

                        @body("tensor")
                        def _(t, t2=t2, half=half, idx=idx, tag=tag, j=j,
                              wslot=wslot, wsem=wsem):
                            if idx == 0:
                                sV.wait(t, marks[(tag, "xmix", j)])
                                wsem.wait(t, marks[(tag, "wld", j)])
                            if idx >= 2:
                                if j == "v":
                                    sV.wait(t, marks[(tag, "pjev", j, idx - 2)])
                                else:
                                    sA.wait(t, marks[(tag, "pjav", j, idx - 2)])
                            psd = ps0 if t2 % 2 == 0 else ps1
                            for cc in range(FC):
                                li = t.matmul(
                                    psd[:, half * 512:half * 512 + 512],
                                    a_xmix[:, cc * BSPAN + t2 * 128:
                                           cc * BSPAN + t2 * 128 + 128],
                                    wslot[:, cc * 1024 + half * 512:
                                          cc * 1024 + half * 512 + 512],
                                    start=(cc == 0), stop=(cc == FC - 1))
                            sP.inc(li)
                        marks[(tag, "pjmv", j, idx)] = sP.n

                        if j == "v":
                            @body("vector")
                            def _(v, t2=t2, half=half, idx=idx, tag=tag,
                                  dst=dst):
                                sP.wait(v, marks[(tag, "pjmv", "v", idx)])
                                psd = ps0 if t2 % 2 == 0 else ps1
                                sV.inc(v.tensor_copy(
                                    out=dst[:, t2 * 1024 + half * 512:
                                            t2 * 1024 + half * 512 + 512],
                                    in_=psd[:, half * 512:half * 512 + 512]))
                            marks[(tag, "pjev", "v", idx)] = sV.n
                        else:
                            @body("scalar")
                            def _(a, t2=t2, half=half, idx=idx, tag=tag,
                                  dst=dst):
                                sP.wait(a, marks[(tag, "pjmv", "g", idx)])
                                psd = ps0 if t2 % 2 == 0 else ps1
                                sA.inc(a.activation(
                                    out=dst[:, t2 * 1024 + half * 512:
                                            t2 * 1024 + half * 512 + 512],
                                    in_=psd[:, half * 512:half * 512 + 512],
                                    func=AF.Silu))
                            marks[(tag, "pjav", "g", idx)] = sA.n
                marks[(tag, "proj", j)] = sV.n if j == "v" else sA.n
                marks[("wslotfree", wname)] = sP.n

            for j in jlist:
                if BCUT <= 3 + jlist.index(j):
                    return
                emit_mix(j, tag)
                if j == "w":
                    emit_proj_w(tag)
                else:
                    if j in ("k", "r"):
                        emit_proj_fm(j, a_kTp if j == "k" else a_rT, tag)
                    else:
                        emit_proj_tm("v", a_v, tag)
                    if pend["i"] < len(rotseq):
                        emit_wload(rotseq[pend["i"]])
                        pend["i"] += 1

            if BCUT <= 7:
                return
            for c in range(NCH):
                if pas == 1:
                    @body("vector")
                    def _(v, c=c, tag=tag):
                        sV.wait(v)
                        sV.inc(v.tensor_copy(out=a_sbf, in_=s_S[:]))
                    marks[(tag, "sbf", c)] = sV.n
                for hg in range(8):
                    sl = hg % 2
                    csp_s = a_csp[:, sl * CHK:(sl + 1) * CHK]
                    ecsr_s = a_ecsr[:, sl * CHK:(sl + 1) * CHK]
                    ecsk_s = a_ecsk[:, sl * CHK:(sl + 1) * CHK]
                    rt_s = a_rt[:, sl * CHK:(sl + 1) * CHK]
                    kt_s = a_kt[:, sl * CHK:(sl + 1) * CHK]
                    wslice = a_w[:, hg * BSPAN + c * CHK:
                                 hg * BSPAN + c * CHK + CHK]
                    rsl = a_rT[:, hg * BSPAN + c * CHK:
                               hg * BSPAN + c * CHK + CHK]
                    ksl = a_kTp[:, hg * BSPAN + c * CHK:
                                hg * BSPAN + c * CHK + CHK]

                    @body("scalar")
                    def _(a, hg=hg, c=c, tag=tag, wslice=wslice, pas=pas):
                        if hg == 0 and c == 0:
                            sA.wait(a, marks[(tag, "proj", "w")])
                        sV.wait(a)
                        sA.inc(a.activation(out=a_expw, in_=wslice,
                                            func=AF.Exp))
                    marks[(tag, "expw", c, hg)] = sA.n

                    @body("vector")
                    def _(v, hg=hg, c=c, tag=tag, csp_s=csp_s, pas=pas,
                          sp=sp):
                        sA.wait(v, marks[(tag, "expw", c, hg)])
                        sV.wait(v)
                        if (c, hg) == (0, 0) and sp == 0 and pas == 0:
                            sV.inc(v.memset(a_zero, 0.0))
                            sV.wait(v)
                        sV.inc(v.tensor_tensor_scan(
                            out=csp_s, data0=a_expw, data1=a_zero,
                            initial=0.0, op0=ALU.add, op1=ALU.add))
                    marks[(tag, "scan", c, hg)] = sV.n
                    if BHCUT <= 1:
                        continue

                    @body("scalar")
                    def _(a, hg=hg, c=c, tag=tag, csp_s=csp_s, ecsr_s=ecsr_s,
                          ecsk_s=ecsk_s, pas=pas):
                        sV.wait(a, marks[(tag, "scan", c, hg)])
                        sA.wait(a)
                        if pas == 1:
                            a.activation(out=ecsr_s[:, 1:CHK],
                                         in_=csp_s[:, 0:CHK - 1],
                                         func=AF.Exp, scale=-1.0)
                        a.activation(out=a_ecsl[:, hg:hg + 1],
                                     in_=csp_s[:, CHK - 1:CHK],
                                     func=AF.Exp, scale=-1.0)
                        sA.inc(a.activation(out=ecsk_s, in_=csp_s,
                                            func=AF.Exp))
                    marks[(tag, "ecs", c, hg)] = sA.n
                    if BHCUT <= 2:
                        continue

                    @body("vector")
                    def _(v, hg=hg, c=c, tag=tag, ecsr_s=ecsr_s,
                          ecsk_s=ecsk_s, rt_s=rt_s, kt_s=kt_s, rsl=rsl,
                          ksl=ksl, pas=pas):
                        sA.wait(v, marks[(tag, "ecs", c, hg)])
                        sV.wait(v)
                        sP.wait(v)
                        if pas == 1:
                            sV.inc(v.memset(ecsr_s[:, 0:1], 1.0))
                            sV.wait(v)
                            v.tensor_tensor(out=rt_s, in0=rsl, in1=ecsr_s,
                                            op=ALU.mult)
                            v.tensor_tensor(out=kt_s, in0=ksl, in1=ecsk_s,
                                            op=ALU.mult)
                            sV.inc(v.scalar_tensor_tensor(
                                out=a_ruk, in0=rsl,
                                scalar=s_ucol[:, hg:hg + 1],
                                in1=ksl, op0=ALU.mult, op1=ALU.mult))
                        else:
                            sV.inc(v.tensor_tensor(out=kt_s, in0=ksl,
                                                   in1=ecsk_s, op=ALU.mult))
                    marks[(tag, "sct", c, hg)] = sV.n
                    if BHCUT <= 3:
                        continue

                    @body("tensor")
                    def _(t, hg=hg, c=c, tag=tag, rt_s=rt_s, kt_s=kt_s,
                          pas=pas):
                        sV.wait(t, marks[(tag, "sct", c, hg)])
                        sV.wait(t)
                        if pas == 1:
                            for h in range(2):
                                t.matmul(ps2[:, h * 512:h * 512 + 128],
                                         kt_s[64 * h:64 * h + 64, :],
                                         rt_s[64 * h:64 * h + 64, :],
                                         start=True, stop=True)
                            t.transpose(psT[:, 128:256], a_ruk[:],
                                        s_idb[:])
                        li = t.transpose(psT[:, 0:128], kt_s, s_idb[:])
                        sP.inc(li)
                    marks[(tag, "hgm", c, hg)] = sP.n
                    if BHCUT <= 4:
                        continue

                    @body("vector")
                    def _(v, hg=hg, c=c, tag=tag, pas=pas):
                        sP.wait(v, marks[(tag, "hgm", c, hg)])
                        sV.wait(v)
                        if pas == 1:
                            for h in range(2):
                                v.tensor_tensor(
                                    out=a_A[:, h * 256:h * 256 + 128],
                                    in0=ps2[:, h * 512:h * 512 + 128],
                                    in1=s_msk[:], op=ALU.mult)
                            for h in range(2):
                                v.reduce_sum(
                                    out=a_dvc[:, 2 * hg + h:2 * hg + h + 1],
                                    in_=psT[:, 128 + h * 64:128 + h * 64 + 64],
                                    axis=AX.X)
                        sV.inc(v.tensor_copy(out=a_ktt[:], in_=psT[:, 0:128]))
                    marks[(tag, "hge", c, hg)] = sV.n
                    if BHCUT <= 5:
                        continue

                    @body("tensor")
                    def _(t, hg=hg, c=c, tag=tag, rt_s=rt_s, pas=pas,
                          lsup=last_supd["v"]):
                        sV.wait(t, marks[(tag, "hge", c, hg)])
                        sV.wait(t, lsup)
                        if pas == 1:
                            for h in range(2):
                                hh = 2 * hg + h
                                t.matmul(ps0[:, hh * 64:hh * 64 + 64],
                                         a_A[:, h * 256:h * 256 + 128],
                                         a_v[:, c * 1024 + hh * 64:
                                             c * 1024 + hh * 64 + 64],
                                         start=True, stop=False)
                                t.matmul(ps0[:, hh * 64:hh * 64 + 64],
                                         rt_s[64 * h:64 * h + 64, :],
                                         a_sbf[64 * h:64 * h + 64,
                                               hg * 64:hg * 64 + 64],
                                         start=False, stop=True)
                        for h in range(2):
                            li = t.matmul(
                                ps1[64 * h:64 * h + 64, hg * 64:hg * 64 + 64],
                                a_ktt[:, h * 64:h * 64 + 64],
                                a_v[:, c * 1024 + (2 * hg + h) * 64:
                                    c * 1024 + (2 * hg + h) * 64 + 64],
                                start=True, stop=True)
                        sP.inc(li)
                    marks[(tag, "yk", c, hg)] = sP.n
                    if BHCUT <= 6:
                        continue

                    @body("vector")
                    def _(v, hg=hg, c=c, tag=tag):
                        sP.wait(v, marks[(tag, "yk", c, hg)])
                        sV.wait(v)
                        sV.inc(v.tensor_add(out=a_st,
                                            in0=s_S[:, hg * 64:hg * 64 + 64],
                                            in1=ps1[:, hg * 64:hg * 64 + 64]))
                        sV.wait(v)
                        sV.inc(v.tensor_scalar(
                            out=s_S[:, hg * 64:hg * 64 + 64], in0=a_st,
                            scalar1=a_ecsl[:, hg:hg + 1], scalar2=None,
                            op0=ALU.mult))
                    marks[(tag, "supd", c, hg)] = sV.n
                    last_supd["v"] = sV.n

                if pas == 1 and BPCUT > 0:
                    @body("vector")
                    def _(v, c=c, tag=tag):
                        sP.wait(v, marks[(tag, "yk", c, 7)])
                        sV.wait(v)
                        for h in range(NH):
                            li = v.scalar_tensor_tensor(
                                out=a_y[:, h * 64:h * 64 + 64],
                                in0=a_v[:, c * 1024 + h * 64:
                                        c * 1024 + h * 64 + 64],
                                scalar=a_dvc[:, h:h + 1],
                                in1=ps0[:, h * 64:h * 64 + 64],
                                op0=ALU.mult, op1=ALU.add)
                        sV.inc(li)
                        sV.wait(v)
                        y3 = a_y.rearrange("p (h k) -> p h k", h=NH)
                        sV.inc(v.reduce_sum(out=st_sum[:, 0:NH], in_=y3,
                                            axis=AX.X))
                    marks[(tag, "yev", c)] = sV.n
                    if BPCUT <= 1:
                        continue

                    @body("scalar")
                    def _(a, c=c, tag=tag):
                        sV.wait(a, marks[(tag, "yev", c)])
                        sA.wait(a)
                        sA.inc(a.activation(out=tsh[:], in_=a_y,
                                            func=AF.Square))
                    marks[(tag, "ysq", c)] = sA.n

                    @body("vector")
                    def _(v, c=c, tag=tag):
                        sA.wait(v, marks[(tag, "ysq", c)])
                        sV.wait(v)
                        t3 = tsh[:].rearrange("p (h k) -> p h k", h=NH)
                        sV.inc(v.reduce_sum(out=st_sq[:, 0:NH], in_=t3,
                                            axis=AX.X))
                    ln_chain_gn(NH)

                    @body("vector")
                    def _(v, c=c, tag=tag):
                        sV.wait(v)
                        y3 = a_y.rearrange("p (h k) -> p h k", h=NH)
                        mr = st_rstd[:, 0:NH].unsqueeze(2).broadcast_to(
                            (128, NH, HK))
                        br = st_b[:, 0:NH].unsqueeze(2).broadcast_to(
                            (128, NH, HK))
                        sV.inc(v.tensor_tensor(out=y3, in0=y3, in1=mr,
                                               op=ALU.mult))
                        sV.wait(v)
                        sV.inc(v.tensor_tensor(out=y3, in0=y3, in1=br,
                                               op=ALU.add))
                        sV.wait(v)
                        sV.inc(v.tensor_tensor(out=a_y, in0=a_y, in1=a_lng,
                                               op=ALU.mult))
                        sV.wait(v)
                        sV.inc(v.tensor_tensor(
                            out=a_yn[:, c * 1024:(c + 1) * 1024],
                            in0=a_y, in1=a_lnb, op=ALU.add))
                    marks[(tag, "yn", c)] = sV.n
                    if BPCUT <= 2:
                        continue

            if pas == 1:
                if BPCUT <= 3:
                    return
                emit_mix("g", tag)
                emit_proj_tm("g", a_g, tag)
                if BPCUT <= 4:
                    return
                for t2 in range(NT2):
                    @body("vector")
                    def _(v, t2=t2, tag=tag):
                        sV.wait(v)
                        sA.wait(v)
                        sV.inc(v.tensor_tensor(
                            out=a_yg[:, t2 * 1024:(t2 + 1) * 1024],
                            in0=a_yn[:, t2 * 1024:(t2 + 1) * 1024],
                            in1=a_g[:, t2 * 1024:(t2 + 1) * 1024],
                            op=ALU.mult))
                    marks[(tag, "yg", t2)] = sV.n

                    @body("tensor")
                    def _(t, t2=t2, tag=tag):
                        sV.wait(t, marks[(tag, "yg", t2)])
                        sV.wait(t)
                        for cc in range(FC):
                            li = t.transpose(psT[:, cc * 128:(cc + 1) * 128],
                                             a_yg[:, t2 * 1024 + cc * 128:
                                                  t2 * 1024 + cc * 128 + 128],
                                             s_idb[:])
                        sP.inc(li)
                    marks[(tag, "ygT", t2)] = sP.n

                    @body("vector")
                    def _(v, t2=t2, tag=tag):
                        sP.wait(v, marks[(tag, "ygT", t2)])
                        sV.wait(v)
                        sV.inc(v.tensor_copy(
                            out=a_ygT[:, t2 * 1024:(t2 + 1) * 1024],
                            in_=psT[:]))
                    marks[(tag, "ygTe", t2)] = sV.n
                    if BPCUT <= 5:
                        continue

                    @body("tensor")
                    def _(t, t2=t2, tag=tag):
                        sV.wait(t, marks[(tag, "ygTe", t2)])
                        wname, wsem = wrot["o"]
                        if t2 == 0:
                            wsem.wait(t, marks[(tag, "wld", "o")])
                        wslot = wA if wname == "A" else wB
                        psd = ps0 if t2 % 2 == 0 else ps1
                        for half in range(2):
                            for cc in range(FC):
                                li = t.matmul(
                                    psd[:, half * 512:half * 512 + 512],
                                    a_ygT[:, t2 * 1024 + cc * 128:
                                          t2 * 1024 + cc * 128 + 128],
                                    wslot[:, cc * 1024 + half * 512:
                                          cc * 1024 + half * 512 + 512],
                                    start=(cc == 0), stop=(cc == FC - 1))
                        sP.inc(li)
                    marks[(tag, "wom", t2)] = sP.n

                    @body("vector")
                    def _(v, t2=t2, tag=tag, tile0=tile0):
                        sP.wait(v, marks[(tag, "wom", t2)])
                        sV.wait(v)
                        psd = ps0 if t2 % 2 == 0 else ps1
                        tile = tile0 + t2
                        strm = stream[:, tile * 1024:(tile + 1) * 1024]
                        sV.inc(v.tensor_tensor(out=strm, in0=strm,
                                               in1=psd[:], op=ALU.add))
                    marks[(tag, "res", t2)] = sV.n
                marks[("wslotfree", wrot["o"][0])] = sP.n
                marks[("res_prev",)] = sV.n

        for sp in range(min(NBSP, BSPANS)):
            emit_span(0, sp)

        @body("gpsimd")
        def _(g):
            sV.wait(g, last_supd["v"])
            if SIMSAFE:
                sM.inc(g.memset(s_srx[:], 0.0))
                return
            g.reg_alu(RG["roff"], RG["rp"], 128 * 8 * HK, ALU.mult)
            sM.dinc(g.dma_start(
                out=bass.AP(mail_s, RG["roff"],
                            [[8 * HK, 128], [1, 8 * HK]]),
                in_=s_S[:]))
            sM.wait(g)
            g.reg_alu(RG["roff"], RG["rp"], 32, ALU.mult)
            g.reg_alu(RG["roff"], RG["roff"], 16, ALU.add)
            sM.dinc(g.dma_start(out=bass.AP(mail_f, RG["roff"],
                                            [[16, 1], [1, 1]]),
                                in_=s_flag[0:1, 0:1]))
            sM.wait(g)
            g.reg_alu(RG["rq"], RG["rp"], 1, ALU.bitwise_xor)
            g.reg_alu(RG["roff"], RG["rq"], 32, ALU.mult)
            g.reg_alu(RG["roff"], RG["roff"], 16, ALU.add)
            g.reg_mov(RG["rpred"], 1)
            with g.While(RG["rpred"]):
                g.load(RG["rf"], bass.AP(mail_f, RG["roff"], [[16, 1], [1, 1]]))
                g.reg_alu(RG["rpred"], RG["rf"], RG["rnonce"], ALU.is_equal)
                g.reg_alu(RG["rpred"], RG["rpred"], 1, ALU.bitwise_xor)
                g.reg_alu(RG["rpred"], RG["rpred"], RG["rneed"],
                          ALU.bitwise_and)
            g.reg_alu(RG["roff"], RG["rq"], 128 * 8 * HK, ALU.mult)
            sM.dinc(g.dma_start(
                out=s_srx[:],
                in_=bass.AP(mail_s, RG["roff"],
                            [[8 * HK, 128], [1, 8 * HK]])))
            sM.wait(g)
        marks["srx"] = sM.n

        @body("vector")
        def _(v):
            sM.wait(v, marks["srx"])
            sV.wait(v)
            sV.inc(v.memset(s_S[:], 0.0))
            sV.wait(v)
            sV.inc(v.copy_predicated(
                out=s_S[:], mask=s_um[:, 0:1].broadcast_to((128, 8 * HK)),
                data=s_srx[:]))
            sV.wait(v)
            sV.inc(v.tensor_copy(out=s_lastcol[:], in_=a_lc0))
        marks["sinit1"] = sV.n
        last_supd["v"] = marks["sinit1"]

        if BPASS1:
            for sp in range(min(NBSP, BSPANS)):
                emit_span(1, sp)

        marks[("w1free_prev",)] = sP.n
        marks[("w2free_prev",)] = sP.n

    if NSUB >= 3:
        emit_attn_b()
    if NSUB >= 4:
        emit_ffn(1)

    def emit_store():
        @body("sync")
        def _(y):
            sV.wait(y)
            sA.wait(y)
            sP.wait(y)
            sM.wait(y)
            sO.dinc(y.dma_start(
                out=y_out.ap().rearrange("(t p) f -> p t f", p=128),
                in_=stream[:].rearrange("p (t f) -> p t f", t=NT)))
            sO.wait(y)

    emit_store()
    ctx.close()
    return nc


def _layouts(kw):
    f = np.float32

    def cc(x):
        return np.ascontiguousarray(np.asarray(x), dtype=f)

    L = {}
    L["wqT"] = cc(np.asarray(kw["L0_attn_a_Wq"]).T)
    L["wkT"] = cc(np.asarray(kw["L0_attn_a_Wk"]).T)
    L["wvT"] = cc(np.asarray(kw["L0_attn_a_Wv"]).T)
    L["ws1"] = cc(kw["L0_ffn_ws1"])
    L["ws2"] = cc(kw["L0_ffn_ws2"])
    L["we1"] = cc(kw["L0_ffn_We1"])
    L["we2"] = cc(kw["L0_ffn_We2"])
    L["bs1c"] = cc(np.asarray(kw["L0_ffn_bs1"]).reshape(FC, 128).T)
    L["be1c"] = cc(np.asarray(kw["L0_ffn_Be1"]).reshape(NE, FC, 128)
                   .transpose(2, 0, 1).reshape(128, NE * FC))
    L["bs2r"] = cc(np.asarray(kw["L0_ffn_bs2"]).reshape(1, C))
    L["be2"] = cc(kw["L0_ffn_Be2"])
    L["wrf"] = cc(kw["L0_ffn_Wr"])
    L["brr"] = cc(np.asarray(kw["L0_ffn_br"]).reshape(1, NE))
    L["wrT"] = cc(np.asarray(kw["L0_attn_b_Wr"]).T)
    L["wkbT"] = cc(np.asarray(kw["L0_attn_b_Wk"]).T)
    L["wvbT"] = cc(np.asarray(kw["L0_attn_b_Wv"]).T)
    L["wgT"] = cc(np.asarray(kw["L0_attn_b_Wg"]).T)
    L["woT"] = cc(np.asarray(kw["L0_attn_b_Wo"]).T)
    L["maa_w1"] = cc(np.asarray(kw["L0_attn_b_maa_w1"]).reshape(FC, 128, 160)
                     .transpose(1, 0, 2).reshape(128, FC * 160))
    L["maa_w2"] = cc(np.asarray(kw["L0_attn_b_maa_w2"]).transpose(1, 0, 2)
                     .reshape(32, 5 * C))
    L["td_w1"] = cc(np.asarray(kw["L0_attn_b_td_w1"]).reshape(FC, 128, 64)
                    .transpose(1, 0, 2).reshape(128, FC * 64))
    L["td_w2"] = cc(kw["L0_attn_b_td_w2"])
    for nm, key in (("tmx", "time_maa_x"), ("tmw", "time_maa_w"),
                    ("tmk", "time_maa_k"), ("tmv", "time_maa_v"),
                    ("tmr", "time_maa_r"), ("tmg", "time_maa_g")):
        L[nm] = cc(np.asarray(kw["L0_attn_b_" + key]).reshape(FC, 128).T)
    L["tdc"] = cc(np.asarray(kw["L0_attn_b_time_decay"]).reshape(FC, 128).T)
    L["ucol"] = cc(np.asarray(kw["L0_attn_b_u"]).reshape(8, 2, HK)
                   .transpose(1, 2, 0).reshape(128, 8))
    L["lngr"] = cc(np.asarray(kw["L0_attn_b_ln_g"]).reshape(1, C))
    L["lnbr"] = cc(np.asarray(kw["L0_attn_b_ln_b"]).reshape(1, C))
    return L


def kernel(**kw):
    import time as _time
    if "nc" not in _CACHE:
        _CACHE["nc"] = _build()
    nc = _CACHE["nc"]
    L = _layouts(kw)
    x = np.ascontiguousarray(np.asarray(kw["L0_attn_a_x"]), dtype=np.float32)
    nonce = int(_time.time_ns() & 0x3FFFFFFF) | 1
    in_maps = []
    for c in range(8):
        b, h = c // 2, c % 2
        if h:
            xb = np.ascontiguousarray(
                np.concatenate([x[b, TL:], x[b, :TL]], axis=0))
        else:
            xb = np.ascontiguousarray(x[b])
        prm = np.array([[h, h, nonce, 0, 0, 0, 0, 0]], dtype=np.int32)
        um = np.full((128, 1), h, dtype=np.int32)
        m = {"xb": xb, "prm": prm, "umcol": um}
        m.update(L)
        in_maps.append(m)
    res = run_bass_kernel_spmd(nc, in_maps, core_ids=list(range(8)))
    out = np.empty((B, S, C), dtype=np.float32)
    for c in range(8):
        b, h = c // 2, c % 2
        out[b, h * TL:(h + 1) * TL] = np.asarray(res.results[c]["y"])
    return out
